# version 02
# baseline (speedup 1.0000x reference)
import contextlib
import numpy as np
import concourse.bass as bass
import concourse.mybir as mybir
from concourse.bass_utils import run_bass_kernel_spmd

F32 = mybir.dt.float32
BF16 = mybir.dt.bfloat16
AF = mybir.ActivationFunctionType
ALU = mybir.AluOpType
AX = mybir.AxisListType

NC = 8
B, T, D = 4, 8192, 1024
TH = T // 2
H, HD, R = 16, 64, 4
FF = 4096
NEG = -3.0e38


class Prog:

    def __init__(self):
        self.nc = bass.Bass("TRN2", target_bir_lowering=False)
        self.es = contextlib.ExitStack()
        self.ops = []

    def din(self, name, shape, dt):
        return self.nc.dram_tensor(name, list(shape), dt, kind="ExternalInput").ap()

    def dout(self, name, shape, dt):
        return self.nc.dram_tensor(name, list(shape), dt, kind="ExternalOutput").ap()

    def sb(self, name, shape, dt):
        return self.es.enter_context(self.nc.sbuf_tensor(name, list(shape), dt))

    def ps(self, name, shape, dt=F32):
        return self.es.enter_context(self.nc.psum_tensor(name, list(shape), dt))

    def op(self, eng, *fns):
        self.ops.append((eng, list(fns), 'c'))

    def dma(self, eng, *fns):
        self.ops.append((eng, list(fns), 'd'))

    def finish(self):
        nc, es = self.nc, self.es
        C = es.enter_context(nc.semaphore("Cser"))
        Dm = es.enter_context(nc.semaphore("Dser"))
        block = es.enter_context(nc.Block())
        engines = []
        for e, _, _ in self.ops:
            if e not in engines:
                engines.append(e)
        if 'sync' not in engines:
            engines.append('sync')

        def make_body(ename):
            def body(eng):
                cn = dn = 0
                for (e, fns, kind) in self.ops:
                    if e == ename:
                        if cn:
                            eng.wait_ge(C, cn)
                        if dn:
                            eng.wait_ge(Dm, dn)
                        insts = [fn(eng) for fn in fns]
                        if kind == 'd':
                            for inst in insts:
                                inst.then_inc(Dm, 16)
                        else:
                            insts[-1].then_inc(C, 1)
                    if kind == 'd':
                        dn += 16 * len(fns)
                    else:
                        cn += 1
                if ename == 'sync':
                    eng.wait_ge(C, cn)
                    eng.wait_ge(Dm, dn)
            return body

        for ename in engines:
            getattr(block, ename)(make_body(ename))
        es.close()
        return nc


def _run(p, in_maps):
    nc = p.finish()
    res = run_bass_kernel_spmd(nc, in_maps, core_ids=list(range(NC)))
    return res.results


def build_ln(xname="x", oname="xn", width=TH, extra_out_x1=False):
    p = Prog()
    NT = width // 512
    x = p.din(xname, (D, width), F32)
    if extra_out_x1:
        s_in = p.din("s", (D, width), F32)
        g_in = p.din("g", (D, width), F32)
        x1_out = p.dout("x1", (D, width), F32)
    xn = p.dout(oname, (D, width), BF16)

    xt = [p.sb(f"xt{c}", (128, 512), F32) for c in range(8)]
    if extra_out_x1:
        st_ = [p.sb(f"st{c}", (128, 512), F32) for c in range(8)]
        gt = [p.sb(f"gt{c}", (128, 512), F32) for c in range(8)]
    xsq = p.sb("xsq", (128, 512), F32)
    xnb = [p.sb(f"xnb{c}", (128, 512), BF16) for c in range(8)]
    ones_col = p.sb("ones_col", (128, 1), F32)
    ones_row = p.sb("ones_row", (1, 128), F32)
    eps_col = p.sb("eps_col", (1, 1), F32)
    mrow = p.sb("mrow", (1, 512), F32)
    m2row = p.sb("m2row", (1, 512), F32)
    tmpr = p.sb("tmpr", (1, 512), F32)
    varr = p.sb("varr", (1, 512), F32)
    lnv = p.sb("lnv", (1, 512), F32)
    rrow = p.sb("rrow", (1, 512), F32)
    nmr = p.sb("nmr", (1, 512), F32)
    t1 = p.sb("t1", (128, 512), F32)
    ps_s = p.ps("ps_s", (1, 512))
    ps_q = p.ps("ps_q", (1, 512))
    ps_r = p.ps("ps_r", (128, 512))
    ps_n = p.ps("ps_n", (128, 512))

    p.op('gpsimd',
         lambda g: g.memset(ones_col[:, :], 1.0),
         lambda g: g.memset(ones_row[:, :], 1.0),
         lambda g: g.memset(eps_col[:, :], 1e-5))

    for tt in range(NT):
        sl = slice(tt * 512, (tt + 1) * 512)
        p.dma('sync', *[(lambda c_, sl_: lambda e: e.dma_start(out=xt[c_][:, :], in_=x[c_ * 128:(c_ + 1) * 128, sl_]))(c, sl) for c in range(8)])
        if extra_out_x1:
            p.dma('sync', *[(lambda c_, sl_: lambda e: e.dma_start(out=st_[c_][:, :], in_=s_in[c_ * 128:(c_ + 1) * 128, sl_]))(c, sl) for c in range(8)])
            p.dma('sync', *[(lambda c_, sl_: lambda e: e.dma_start(out=gt[c_][:, :], in_=g_in[c_ * 128:(c_ + 1) * 128, sl_]))(c, sl) for c in range(8)])
            p.op('vector', *[(lambda c_: lambda v: v.tensor_tensor(st_[c_][:, :], st_[c_][:, :], gt[c_][:, :], ALU.mult))(c) for c in range(8)])
            p.op('vector', *[(lambda c_: lambda v: v.tensor_tensor(xt[c_][:, :], xt[c_][:, :], st_[c_][:, :], ALU.add))(c) for c in range(8)])
            p.dma('gpsimd', *[(lambda c_, sl_: lambda e: e.dma_start(out=x1_out[c_ * 128:(c_ + 1) * 128, sl_], in_=xt[c_][:, :]))(c, sl) for c in range(8)])
        p.op('tensor', *[(lambda c_: lambda t: t.matmul(ps_s[:, :], ones_col[:, :], xt[c_][:, :], start=(c_ == 0), stop=(c_ == 7)))(c) for c in range(8)])
        for c in range(8):
            p.op('scalar', (lambda c_: lambda s: s.activation(xsq[:, :], xt[c_][:, :], AF.Square))(c))
            p.op('tensor', (lambda c_: lambda t: t.matmul(ps_q[:, :], ones_col[:, :], xsq[:, :], start=(c_ == 0), stop=(c_ == 7)))(c))
        p.op('vector',
             lambda v: v.tensor_scalar(mrow[:, :], ps_s[:, :], 1.0 / D, None, ALU.mult),
             lambda v: v.tensor_scalar(m2row[:, :], ps_q[:, :], 1.0 / D, None, ALU.mult))
        p.op('vector', lambda v: v.tensor_tensor(tmpr[:, :], mrow[:, :], mrow[:, :], ALU.mult))
        p.op('vector', lambda v: v.tensor_tensor(varr[:, :], m2row[:, :], tmpr[:, :], ALU.subtract))
        p.op('scalar', lambda s: s.activation(lnv[:, :], varr[:, :], AF.Ln, bias=eps_col[:, :]))
        p.op('scalar', lambda s: s.activation(rrow[:, :], lnv[:, :], AF.Exp, scale=-0.5))
        p.op('vector', lambda v: v.scalar_tensor_tensor(nmr[:, :], mrow[:, :], -1.0, rrow[:, :], ALU.mult, ALU.mult))
        p.op('tensor',
             lambda t: t.matmul(ps_r[:, :], ones_row[:, :], rrow[:, :], start=True, stop=True),
             lambda t: t.matmul(ps_n[:, :], ones_row[:, :], nmr[:, :], start=True, stop=True))
        for c in range(8):
            p.op('vector', (lambda c_: lambda v: v.tensor_tensor(t1[:, :], xt[c_][:, :], ps_r[:, :], ALU.mult))(c))
            p.op('vector', (lambda c_: lambda v: v.tensor_tensor(xnb[c_][:, :], t1[:, :], ps_n[:, :], ALU.add))(c))
        p.dma('gpsimd', *[(lambda c_, sl_: lambda e: e.dma_start(out=xn[c_ * 128:(c_ + 1) * 128, sl_], in_=xnb[c_][:, :]))(c, sl) for c in range(8)])
    return p


def stage1_ln0(x_fm):
    p = build_ln()
    in_maps = []
    for c in range(NC):
        b, half = c // 2, c % 2
        in_maps.append({"x": np.ascontiguousarray(x_fm[b][:, half * TH:(half + 1) * TH])})
    res = _run(p, in_maps)
    xn0 = np.empty((B, D, T), dtype=res[0]["xn"].dtype)
    for c in range(NC):
        b, half = c // 2, c % 2
        xn0[b][:, half * TH:(half + 1) * TH] = res[c]["xn"]
    return xn0


def build_l0mm():
    p = Prog()
    xn = p.din("xn", (D, T), BF16)
    wk = p.din("wk", (D, 512), F32)
    wv = p.din("wv", (D, 512), F32)
    bkr = p.din("bkr", (128, 4), F32)
    bvr = p.din("bvr", (128, 4), F32)
    dotw = p.din("dotw", (128, 512), F32)
    v1o = p.dout("v1", (4, 128, T), F32)
    vldo = p.dout("vld", (4, 128, T), BF16)

    xnt = [p.sb(f"xnt{c}", (128, T), BF16) for c in range(8)]
    wstage = p.sb("wstage", (128, 512), F32)
    wkb = [p.sb(f"wkb{c}", (128, 512), BF16) for c in range(8)]
    wvb = [p.sb(f"wvb{c}", (128, 512), BF16) for c in range(8)]
    bks = p.sb("bks", (128, 4), F32)
    bvs = p.sb("bvs", (128, 4), F32)
    dws = p.sb("dws", (128, 512), F32)
    k1 = [p.sb(f"k1_{c}", (128, 512), F32) for c in range(4)]
    v1s = [p.sb(f"v1s{c}", (128, 512), F32) for c in range(4)]
    vldb = [p.sb(f"vldb{c}", (128, 512), BF16) for c in range(4)]
    psKV = [p.ps(f"ps{i}", (128, 512)) for i in range(8)]

    p.dma('sync', *[(lambda c_: lambda e: e.dma_start(out=xnt[c_][:, :], in_=xn[c_ * 128:(c_ + 1) * 128, :]))(c) for c in range(8)])
    p.dma('sync',
          lambda e: e.dma_start(out=bks[:, :], in_=bkr),
          lambda e: e.dma_start(out=bvs[:, :], in_=bvr),
          lambda e: e.dma_start(out=dws[:, :], in_=dotw))
    for c in range(8):
        p.dma('sync', (lambda c_: lambda e: e.dma_start(out=wstage[:, :], in_=wk[c_ * 128:(c_ + 1) * 128, :]))(c))
        p.op('vector', (lambda c_: lambda v: v.tensor_copy(wkb[c_][:, :], wstage[:, :]))(c))
        p.dma('sync', (lambda c_: lambda e: e.dma_start(out=wstage[:, :], in_=wv[c_ * 128:(c_ + 1) * 128, :]))(c))
        p.op('vector', (lambda c_: lambda v: v.tensor_copy(wvb[c_][:, :], wstage[:, :]))(c))

    for tt in range(T // 512):
        sl = slice(tt * 512, (tt + 1) * 512)
        for c4 in range(4):
            p.op('tensor', *[(lambda c4_, d_, sl_: lambda t: t.matmul(
                psKV[c4_][:, :], wkb[d_][:, c4_ * 128:(c4_ + 1) * 128], xnt[d_][:, sl_],
                start=(d_ == 0), stop=(d_ == 7)))(c4, d, sl) for d in range(8)])
        for c4 in range(4):
            p.op('tensor', *[(lambda c4_, d_, sl_: lambda t: t.matmul(
                psKV[4 + c4_][:, :], wvb[d_][:, c4_ * 128:(c4_ + 1) * 128], xnt[d_][:, sl_],
                start=(d_ == 0), stop=(d_ == 7)))(c4, d, sl) for d in range(8)])
        p.op('scalar', *[(lambda c4_: lambda s: s.activation(k1[c4_][:, :], psKV[c4_][:, :], AF.Relu, bias=bks[:, c4_:c4_ + 1]))(c4) for c4 in range(4)])
        p.op('scalar', *[(lambda c4_: lambda s: s.activation(v1s[c4_][:, :], psKV[4 + c4_][:, :], AF.Relu, bias=bvs[:, c4_:c4_ + 1]))(c4) for c4 in range(4)])
        p.dma('gpsimd', *[(lambda c4_, sl_: lambda e: e.dma_start(out=v1o[c4_, :, sl_], in_=v1s[c4_][:, :]))(c4, sl) for c4 in range(4)])
        p.op('tensor', *[(lambda c4_: lambda t: t.matmul(psKV[c4_][:, :], dws[:, c4_ * 128:(c4_ + 1) * 128], k1[c4_][:, :], start=True, stop=True))(c4) for c4 in range(4)])
        p.op('vector', *[(lambda c4_: lambda v: v.tensor_scalar(vldb[c4_][:, :], psKV[c4_][:, :], 0.5, None, ALU.is_gt))(c4) for c4 in range(4)])
        p.dma('gpsimd', *[(lambda c4_, sl_: lambda e: e.dma_start(out=vldo[c4_, :, sl_], in_=vldb[c4_][:, :]))(c4, sl) for c4 in range(4)])
    return p


def stage2_l0mm(xn0, Wk, bk, Wv, bv, RH):
    p = build_l0mm()
    in_maps = []
    for c in range(NC):
        b, half = c // 2, c % 2
        hsl = slice(half * 512, (half + 1) * 512)
        dotw = np.zeros((128, 512), np.float32)
        for c4 in range(4):
            chunk = half * 4 + c4
            for j in range(2):
                head = 2 * chunk + j
                blk = np.broadcast_to(RH[head][:, None], (64, 64))
                dotw[j * 64:(j + 1) * 64, c4 * 128 + j * 64:c4 * 128 + (j + 1) * 64] = blk
        in_maps.append({
            "xn": np.ascontiguousarray(xn0[b]),
            "wk": np.ascontiguousarray(Wk[:, hsl]),
            "wv": np.ascontiguousarray(Wv[:, hsl]),
            "bkr": np.ascontiguousarray(bk[hsl].reshape(4, 128).T),
            "bvr": np.ascontiguousarray(bv[hsl].reshape(4, 128).T),
            "dotw": dotw,
        })
    res = _run(p, in_maps)
    return ([res[c]["v1"] for c in range(NC)], [res[c]["vld"] for c in range(NC)])


def build_scan():
    p = Prog()
    W = T + 2
    v1 = p.din("v1", (4, 128, T), F32)
    vld = p.din("vld", (4, 128, T), BF16)
    wrep = p.din("wrep", (4, 128, 8), F32)
    so = p.dout("s", (4, 128, T), F32)

    Vb = p.sb("Vb", (128, W), F32)
    Fa = p.sb("Fa", (128, W), F32)
    Fb = p.sb("Fb", (128, W), F32)
    bb = p.sb("bb", (128, T), F32)
    sacc = p.sb("sacc", (128, T), F32)
    valF = p.sb("valF", (128, T), BF16)
    nvalF = p.sb("nvalF", (128, T), BF16)
    wcols = p.sb("wcols", (128, 8), F32)
    wbsum = p.sb("wbsum", (128, 1), F32)
    c1 = p.sb("c1", (128, 1), F32)
    c2 = p.sb("c2", (128, 1), F32)
    initc = p.sb("initc", (128, 1), F32)
    s0col = p.sb("s0col", (128, 1), F32)

    def rev(t_, off, n):
        return bass.AP(t_, off, [[t_.shape[1], 128], [-1, n]])

    for ti in range(4):
        p.dma('sync',
              (lambda ti_: lambda e: e.dma_start(out=Vb[:, 1:T + 1], in_=v1[ti_, :, :]))(ti),
              (lambda ti_: lambda e: e.dma_start(out=valF[:, :], in_=vld[ti_, :, :]))(ti),
              (lambda ti_: lambda e: e.dma_start(out=wcols[:, :], in_=wrep[ti_, :, :]))(ti))
        p.op('gpsimd',
             lambda g: g.memset(Vb[:, 0:1], 0.0),
             lambda g: g.memset(Vb[:, T + 1:T + 2], 0.0),
             lambda g: g.memset(Fa[:, 0:1], 0.0),
             lambda g: g.memset(Fa[:, T + 1:T + 2], 0.0),
             lambda g: g.memset(Fb[:, 0:1], 0.0),
             lambda g: g.memset(Fb[:, T + 1:T + 2], 0.0),
             lambda g: g.memset(valF[:, 0:1], 0.0),
             lambda g: g.memset(sacc[:, :], 0.0))
        p.op('vector',
             lambda v: v.tensor_scalar(nvalF[:, :], valF[:, :], -1.0, 1.0, ALU.mult, ALU.add),
             lambda v: v.tensor_tensor(c1[:, :], valF[:, T - 1:T], Vb[:, T:T + 1], ALU.mult),
             lambda v: v.tensor_reduce(wbsum[:, :], wcols[:, 4:8], AX.X, ALU.add))
        p.op('vector', lambda v: v.tensor_tensor(c2[:, :], nvalF[:, T - 1:T], Vb[:, 1:2], ALU.mult))
        p.op('vector', lambda v: v.tensor_tensor(initc[:, :], c1[:, :], c2[:, :], ALU.add))
        src = {0: Vb, 1: None}
        bufs = [Fa, Fb, Fa, Fb]
        prev = None
        for r in range(4):
            if r == 0:
                p.op('vector', lambda v: v.tensor_tensor(bb[:, :], valF[:, :], Vb[:, 1:T + 1], ALU.mult))
            else:
                p.op('vector', (lambda pr: lambda v: v.tensor_tensor(bb[:, :], valF[:, :], pr[:, 0:T], ALU.mult))(prev))
            F = bufs[r]
            p.op('vector', (lambda F_: lambda v: v.tensor_tensor_scan(F_[:, 1:T + 1], nvalF[:, :], bb[:, :], Vb[:, 1:2], ALU.mult, ALU.add))(F))
            p.op('vector', (lambda F_, r_: lambda v: v.scalar_tensor_tensor(sacc[:, :], F_[:, 1:T + 1], wcols[:, r_:r_ + 1], sacc[:, :], ALU.mult, ALU.add))(F, r))
            prev = F
        p.op('vector', lambda v: v.tensor_copy(s0col[:, :], sacc[:, 0:1]))
        p.op('gpsimd',
             lambda g: g.memset(valF[:, T - 1:T], 0.0),
             lambda g: g.memset(nvalF[:, T - 1:T], 1.0))
        prev = None
        for r in range(4):
            if r == 0:
                p.op('vector', lambda v: v.tensor_tensor(bb[:, :], valF[:, :], Vb[:, 2:T + 2], ALU.mult))
                init_ap = initc[:, :]
            else:
                p.op('vector', (lambda pr: lambda v: v.tensor_tensor(bb[:, :], valF[:, :], pr[:, 2:T + 2], ALU.mult))(prev))
                init_ap = Vb[:, 1:2]
            F = bufs[r]
            p.op('vector', (lambda F_, ia: lambda v: v.tensor_tensor_scan(
                rev(F_, T, T), rev(nvalF, T - 1, T), rev(bb, T - 1, T), ia, ALU.mult, ALU.add))(F, init_ap))
            p.op('vector', (lambda F_, r_: lambda v: v.scalar_tensor_tensor(sacc[:, :], F_[:, 1:T + 1], wcols[:, 4 + r_:5 + r_], sacc[:, :], ALU.mult, ALU.add))(F, r))
            prev = F
        p.op('vector', lambda v: v.tensor_tensor(c1[:, :], wbsum[:, :], Vb[:, 1:2], ALU.mult))
        p.op('vector', lambda v: v.tensor_tensor(c2[:, :], c1[:, :], s0col[:, :], ALU.add))
        p.op('vector', lambda v: v.tensor_copy(sacc[:, 0:1], c2[:, :]))
        p.dma('gpsimd', (lambda ti_: lambda e: e.dma_start(out=so[ti_, :, :], in_=sacc[:, :]))(ti))
    return p


def stage3_scan(v1_list, vld_list, bw):
    p = build_scan()
    in_maps = []
    for c in range(NC):
        b, half = c // 2, c % 2
        wrep = np.zeros((4, 128, 8), np.float32)
        for ti in range(4):
            chunk = half * 4 + ti
            for j in range(2):
                wrep[ti, j * 64:(j + 1) * 64, :] = bw[2 * chunk + j][None, :]
        in_maps.append({"v1": v1_list[c], "vld": vld_list[c], "wrep": wrep})
    res = _run(p, in_maps)
    return [res[c]["s"] for c in range(NC)]


def build_ffn0():
    p = Prog()
    xnh = p.din("xnh", (D, TH), BF16)
    wfcT = p.din("wfcT", (D, FF), F32)
    wprojT = p.din("wprojT", (FF, D), F32)
    go = p.dout("g", (D, TH), F32)

    wstage = p.sb("wstage", (128, FF), F32)
    wfcb = [p.sb(f"wfcb{c}", (128, FF), BF16) for c in range(8)]
    wprjb = [p.sb(f"wprjb{c}", (128, D), BF16) for c in range(32)]
    xnt = [p.sb(f"xnt{c}", (128, 512), BF16) for c in range(8)]
    h2 = [p.sb(f"h2_{c}", (128, 512), BF16) for c in range(32)]
    htmp = p.sb("htmp", (128, 512), F32)
    yd = p.sb("yd", (128, 512), F32)
    mem = p.sb("mem", (128, 512), F32)
    spk = p.sb("spk", (128, 512), F32)
    acc = p.sb("acc", (128, 512), F32)
    gd = p.sb("gd", (128, 512), F32)
    psH = p.ps("psH", (128, 512))
    psY = p.ps("psY", (128, 512))

    for c in range(8):
        p.dma('sync', (lambda c_: lambda e: e.dma_start(out=wstage[:, :], in_=wfcT[c_ * 128:(c_ + 1) * 128, :]))(c))
        p.op('vector', (lambda c_: lambda v: v.tensor_copy(wfcb[c_][:, :], wstage[:, :]))(c))
    for c in range(32):
        p.dma('sync', (lambda c_: lambda e: e.dma_start(out=wstage[:, 0:D], in_=wprojT[c_ * 128:(c_ + 1) * 128, :]))(c))
        p.op('vector', (lambda c_: lambda v: v.tensor_copy(wprjb[c_][:, :], wstage[:, 0:D]))(c))

    for tt in range(TH // 512):
        sl = slice(tt * 512, (tt + 1) * 512)
        p.dma('sync', *[(lambda c_, sl_: lambda e: e.dma_start(out=xnt[c_][:, :], in_=xnh[c_ * 128:(c_ + 1) * 128, sl_]))(c, sl) for c in range(8)])
        for fc in range(32):
            p.op('tensor', *[(lambda fc_, d_: lambda t: t.matmul(
                psH[:, :], wfcb[d_][:, fc_ * 128:(fc_ + 1) * 128], xnt[d_][:, :],
                start=(d_ == 0), stop=(d_ == 7)))(fc, d) for d in range(8)])
            p.op('scalar', (lambda: lambda s: s.activation(htmp[:, :], psH[:, :], AF.Relu))())
            p.op('scalar', (lambda fc_: lambda s: s.activation(h2[fc_][:, :], htmp[:, :], AF.Square))(fc))
        for dc in range(8):
            p.op('tensor', *[(lambda dc_, f_: lambda t: t.matmul(
                psY[:, :], wprjb[f_][:, dc_ * 128:(dc_ + 1) * 128], h2[f_][:, :],
                start=(f_ == 0), stop=(f_ == 31)))(dc, f) for f in range(32)])
            p.op('vector',
                 lambda v: v.tensor_copy(yd[:, :], psY[:, :]),
                 lambda v: v.tensor_copy(mem[:, :], psY[:, :]),
                 lambda v: v.tensor_scalar(spk[:, :], psY[:, :], 1.0, None, ALU.is_gt))
            p.op('vector', lambda v: v.tensor_copy(acc[:, :], spk[:, :]))
            for k in range(4):
                p.op('vector', lambda v: v.scalar_tensor_tensor(mem[:, :], mem[:, :], 0.9, yd[:, :], ALU.mult, ALU.add))
                p.op('vector', lambda v: v.tensor_tensor(mem[:, :], mem[:, :], spk[:, :], ALU.subtract))
                p.op('vector', lambda v: v.tensor_scalar(spk[:, :], mem[:, :], 1.0, None, ALU.is_gt))
                p.op('vector', lambda v: v.tensor_tensor(acc[:, :], acc[:, :], spk[:, :], ALU.add))
            p.op('scalar', lambda s: s.activation(gd[:, :], acc[:, :], AF.Silu, scale=0.2))
            p.dma('gpsimd', (lambda dc_, sl_: lambda e: e.dma_start(out=go[dc_ * 128:(dc_ + 1) * 128, sl_], in_=gd[:, :]))(dc, sl))
    return p


def stage4_ffn0(xn0, WfcT, WprojT):
    p = build_ffn0()
    in_maps = []
    for c in range(NC):
        b, half = c // 2, c % 2
        in_maps.append({
            "xnh": np.ascontiguousarray(xn0[b][:, half * TH:(half + 1) * TH]),
            "wfcT": WfcT, "wprojT": WprojT,
        })
    res = _run(p, in_maps)
    return [res[c]["g"] for c in range(NC)]


def stage5_comb1(x_fm, s_fm_slices, g_slices):
    p = build_ln(oname="xn", extra_out_x1=True)
    in_maps = []
    for c in range(NC):
        b, half = c // 2, c % 2
        in_maps.append({
            "x": np.ascontiguousarray(x_fm[b][:, half * TH:(half + 1) * TH]),
            "s": s_fm_slices[c],
            "g": g_slices[c],
        })
    res = _run(p, in_maps)
    return ([res[c]["x1"] for c in range(NC)], [res[c]["xn"] for c in range(NC)])


def build_qkv():
    p = Prog()
    xnh = p.din("xnh", (D, TH), BF16)
    wqT = p.din("wqT", (D, FF), F32)
    wkT = p.din("wkT", (D, FF), F32)
    wvT = p.din("wvT", (D, D), F32)
    qTo = p.dout("qT", (FF, TH), BF16)
    kTo = p.dout("kT", (FF, TH), BF16)
    vo = p.dout("v", (TH, D), BF16)

    wstage = p.sb("wstage", (128, FF), F32)
    wqb = [p.sb(f"wqb{c}", (128, FF), BF16) for c in range(8)]
    wkb = [p.sb(f"wkb{c}", (128, FF), BF16) for c in range(8)]
    wvb = [p.sb(f"wvb{c}", (128, D), BF16) for c in range(8)]
    xnt = [p.sb(f"xnt{c}", (128, 512), BF16) for c in range(8)]
    ev = p.sb("ev", (128, 512), BF16)
    ps0 = p.ps("ps0", (128, 512))
    ps1 = p.ps("ps1", (128, 512))

    for c in range(8):
        p.dma('sync', (lambda c_: lambda e: e.dma_start(out=wstage[:, :], in_=wqT[c_ * 128:(c_ + 1) * 128, :]))(c))
        p.op('vector', (lambda c_: lambda v: v.tensor_copy(wqb[c_][:, :], wstage[:, :]))(c))
        p.dma('sync', (lambda c_: lambda e: e.dma_start(out=wstage[:, :], in_=wkT[c_ * 128:(c_ + 1) * 128, :]))(c))
        p.op('vector', (lambda c_: lambda v: v.tensor_copy(wkb[c_][:, :], wstage[:, :]))(c))
        p.dma('sync', (lambda c_: lambda e: e.dma_start(out=wstage[:, 0:D], in_=wvT[c_ * 128:(c_ + 1) * 128, :]))(c))
        p.op('vector', (lambda c_: lambda v: v.tensor_copy(wvb[c_][:, :], wstage[:, 0:D]))(c))

    for tt in range(TH // 512):
        sl = slice(tt * 512, (tt + 1) * 512)
        p.dma('sync', *[(lambda c_, sl_: lambda e: e.dma_start(out=xnt[c_][:, :], in_=xnh[c_ * 128:(c_ + 1) * 128, sl_]))(c, sl) for c in range(8)])
        for wb, out in ((wqb, qTo), (wkb, kTo)):
            for hc in range(32):
                ps = ps0 if hc % 2 == 0 else ps1
                p.op('tensor', *[(lambda ps_, wb_, hc_, d_: lambda t: t.matmul(
                    ps_[:, :], wb_[d_][:, hc_ * 128:(hc_ + 1) * 128], xnt[d_][:, :],
                    start=(d_ == 0), stop=(d_ == 7)))(ps, wb, hc, d) for d in range(8)])
                p.op('vector', (lambda ps_: lambda v: v.tensor_copy(ev[:, :], ps_[:, :]))(ps))
                p.dma('gpsimd', (lambda out_, hc_, sl_: lambda e: e.dma_start(out=out_[hc_ * 128:(hc_ + 1) * 128, sl_], in_=ev[:, :]))(out, hc, sl))
        for tc in range(4):
            for oh in range(2):
                ps = ps0 if oh == 0 else ps1
                p.op('tensor', *[(lambda ps_, tc_, oh_, d_: lambda t: t.matmul(
                    ps_[:, :], xnt[d_][:, tc_ * 128:(tc_ + 1) * 128], wvb[d_][:, oh_ * 512:(oh_ + 1) * 512],
                    start=(d_ == 0), stop=(d_ == 7)))(ps, tc, oh, d) for d in range(8)])
                p.op('vector', (lambda ps_: lambda v: v.tensor_copy(ev[:, :], ps_[:, :]))(ps))
                p.dma('gpsimd', (lambda tc_, oh_, sl_: lambda e: e.dma_start(
                    out=vo[sl_.start + tc_ * 128:sl_.start + (tc_ + 1) * 128, oh_ * 512:(oh_ + 1) * 512], in_=ev[:, :]))(tc, oh, sl))
    return p


def stage6_qkv(xn1_slices, WqT, WkT, WvT):
    p = build_qkv()
    in_maps = []
    for c in range(NC):
        in_maps.append({"xnh": xn1_slices[c], "wqT": WqT, "wkT": WkT, "wvT": WvT})
    res = _run(p, in_maps)
    return ([res[c]["qT"] for c in range(NC)], [res[c]["kT"] for c in range(NC)],
            [res[c]["v"] for c in range(NC)])


def build_flash():
    p = Prog()
    kT = p.din("kT", (FF, TH), BF16)
    qT = p.din("qT", (FF, T), BF16)
    vt = p.din("v", (T, D), BF16)
    oo = p.dout("o", (TH, D), F32)

    NG = 4
    GI = 8
    NJ = T // 512

    kts = [p.sb(f"kts{c}", (128, 1024), BF16) for c in range(32)]
    qts = [p.sb(f"qts{c}", (128, 512), BF16) for c in range(32)]
    vts = [p.sb(f"vts{c}", (128, 1024), BF16) for c in range(4)]
    Oacc = [p.sb(f"Oacc{i}", (128, 1024), F32) for i in range(GI)]
    mcols = p.sb("mcols", (128, GI), F32)
    lcols = p.sb("lcols", (128, GI), F32)
    mt = p.sb("mt", (128, 1), F32)
    mn = p.sb("mn", (128, 1), F32)
    dif = p.sb("dif", (128, 1), F32)
    bP = p.sb("bP", (128, 1), F32)
    acol = p.sb("acol", (128, 1), F32)
    rs = p.sb("rs", (128, 1), F32)
    linv = p.sb("linv", (128, 1), F32)
    p_sb = p.sb("p_sb", (128, 512), BF16)
    pt_sb = p.sb("pt_sb", (128, 512), BF16)
    ident = p.sb("ident", (128, 128), BF16)
    ident_i = p.sb("ident_i", (128, 128), mybir.dt.int32)
    ob = p.sb("ob", (128, 1024), F32)
    psS = p.ps("psS", (128, 512))
    psT = p.ps("psT", (128, 512), BF16)
    psO = [p.ps(f"psO{i}", (128, 512)) for i in range(2)]

    p.op('gpsimd', lambda g: g.iota(ident_i[:, :], [[1, 128]], base=0, channel_multiplier=-1))
    p.op('vector', lambda v: v.tensor_scalar(ident[:, :], ident_i[:, :], 0, None, ALU.is_equal))

    for gi in range(NG):
        p.dma('sync', *[(lambda c_, gi_: lambda e: e.dma_start(out=kts[c_][:, :], in_=kT[c_ * 128:(c_ + 1) * 128, gi_ * 1024:(gi_ + 1) * 1024]))(c, gi) for c in range(32)])
        p.op('gpsimd',
             *[(lambda i_: lambda g: g.memset(Oacc[i_][:, :], 0.0))(i) for i in range(GI)],
             lambda g: g.memset(mcols[:, :], NEG),
             lambda g: g.memset(lcols[:, :], 0.0))
        for j in range(NJ):
            p.dma('sync', *[(lambda c_, j_: lambda e: e.dma_start(out=qts[c_][:, :], in_=qT[c_ * 128:(c_ + 1) * 128, j_ * 512:(j_ + 1) * 512]))(c, j) for c in range(32)])
            p.dma('sync', *[(lambda c_, j_: lambda e: e.dma_start(out=vts[c_][:, :], in_=vt[j_ * 512 + c_ * 128:j_ * 512 + (c_ + 1) * 128, :]))(c, j) for c in range(4)])
            for i in range(GI):
                p.op('tensor', *[(lambda i_, hc_: lambda t: t.matmul(
                    psS[:, :], kts[hc_][:, i_ * 128:(i_ + 1) * 128], qts[hc_][:, :],
                    start=(hc_ == 0), stop=(hc_ == 31)))(i, hc) for hc in range(32)])
                p.op('vector', lambda v: v.tensor_reduce(mt[:, :], psS[:, :], AX.X, ALU.max))
                p.op('vector', (lambda i_: lambda v: v.tensor_tensor(mn[:, :], mcols[:, i_:i_ + 1], mt[:, :], ALU.max))(i))
                p.op('vector',
                     (lambda i_: lambda v: v.tensor_tensor(dif[:, :], mcols[:, i_:i_ + 1], mn[:, :], ALU.subtract))(i),
                     lambda v: v.tensor_scalar(bP[:, :], mn[:, :], -1.0 / 64.0, None, ALU.mult))
                p.op('vector', (lambda i_: lambda v: v.tensor_copy(mcols[:, i_:i_ + 1], mn[:, :]))(i))
                p.op('scalar',
                     lambda s: s.activation(acol[:, :], dif[:, :], AF.Exp, scale=1.0 / 64.0),
                     lambda s: s.activation(p_sb[:, :], psS[:, :], AF.Exp, bias=bP[:, :], scale=1.0 / 64.0, accum_out=rs[:, :]))
                p.op('vector', (lambda i_: lambda v: v.scalar_tensor_tensor(
                    lcols[:, i_:i_ + 1], lcols[:, i_:i_ + 1], acol[:, :], rs[:, :], ALU.mult, ALU.add))(i))
                p.op('tensor', *[(lambda jc_: lambda t: t.transpose(
                    psT[:, jc_ * 128:(jc_ + 1) * 128], p_sb[:, jc_ * 128:(jc_ + 1) * 128], ident[:, :]))(jc) for jc in range(4)])
                p.op('vector', lambda v: v.tensor_copy(pt_sb[:, :], psT[:, :]))
                p.op('tensor', *[(lambda oh_, jc_: lambda t: t.matmul(
                    psO[oh_][:, :], pt_sb[:, jc_ * 128:(jc_ + 1) * 128], vts[jc_][:, oh_ * 512:(oh_ + 1) * 512],
                    start=(jc_ == 0), stop=(jc_ == 3)))(oh, jc) for oh in range(2) for jc in range(4)])
                p.op('vector', *[(lambda i_, oh_: lambda v: v.scalar_tensor_tensor(
                    Oacc[i_][:, oh_ * 512:(oh_ + 1) * 512], Oacc[i_][:, oh_ * 512:(oh_ + 1) * 512],
                    acol[:, :], psO[oh_][:, :], ALU.mult, ALU.add))(i, oh) for oh in range(2)])
        for i in range(GI):
            p.op('vector', (lambda i_: lambda v: v.reciprocal(linv[:, :], lcols[:, i_:i_ + 1]))(i))
            p.op('scalar', (lambda i_: lambda s: s.activation(ob[:, :], Oacc[i_][:, :], AF.Identity, scale=linv[:, :]))(i))
            p.dma('gpsimd', (lambda gi_, i_: lambda e: e.dma_start(out=oo[gi_ * 1024 + i_ * 128:gi_ * 1024 + (i_ + 1) * 128, :], in_=ob[:, :]))(gi, i))
    return p


def stage7_flash(kT_slices, qT_full, v_full):
    p = build_flash()
    in_maps = []
    for c in range(NC):
        b = c // 2
        in_maps.append({"kT": kT_slices[c], "qT": qT_full[b], "v": v_full[b]})
    res = _run(p, in_maps)
    return [res[c]["o"] for c in range(NC)]


def build_l1ffn2():
    p = Prog()
    xnh = p.din("xnh", (D, TH), BF16)
    x1 = p.din("x1", (D, TH), F32)
    s1 = p.din("s1", (D, TH), F32)
    wqT = p.din("wqT", (D, D), F32)
    rtT = p.din("rtT", (D, 255), F32)
    lvw = p.din("lv", (256, D), F32)
    woT = p.din("woT", (D, D), F32)
    wob = p.din("wob", (128, 8), F32)
    x2o = p.dout("x2", (D, TH), F32)

    wstage = p.sb("wstage", (128, D), F32)
    wqb = [p.sb(f"wqb{c}", (128, D), BF16) for c in range(8)]
    rtb = [p.sb(f"rtb{c}", (128, 255), BF16) for c in range(8)]
    lvb = [p.sb(f"lvb{c}", (128, D), BF16) for c in range(2)]
    wob_s = p.sb("wob_s", (128, 8), F32)
    wobt = [p.sb(f"wobt{c}", (128, D), BF16) for c in range(8)]
    xnt = [p.sb(f"xnt{c}", (128, 512), BF16) for c in range(8)]
    qb = [p.sb(f"qb{c}", (128, 512), BF16) for c in range(8)]
    gg = p.sb("gg", (128, 510), F32)
    lp = p.sb("lp", (128, 256), F32)
    lpb = p.sb("lpb", (128, 256), BF16)
    lpt = [p.sb(f"lpt{m}", (128, 512), BF16) for m in range(2)]
    retb = [p.sb(f"retb{c}", (128, 512), BF16) for c in range(8)]
    g1 = p.sb("g1", (128, 512), F32)
    x1t = p.sb("x1t", (128, 512), F32)
    s1t = p.sb("s1t", (128, 512), F32)
    ident = p.sb("ident", (128, 128), BF16)
    ident_i = p.sb("ident_i", (128, 128), mybir.dt.int32)
    ps0 = p.ps("ps0", (128, 512))
    psg = p.ps("psg", (128, 255))
    pst = p.ps("pst", (128, 512), BF16)

    p.op('gpsimd', lambda g: g.iota(ident_i[:, :], [[1, 128]], base=0, channel_multiplier=-1))
    p.op('vector', lambda v: v.tensor_scalar(ident[:, :], ident_i[:, :], 0, None, ALU.is_equal))
    for c in range(8):
        p.dma('sync', (lambda c_: lambda e: e.dma_start(out=wstage[:, :], in_=wqT[c_ * 128:(c_ + 1) * 128, :]))(c))
        p.op('vector', (lambda c_: lambda v: v.tensor_copy(wqb[c_][:, :], wstage[:, :]))(c))
        p.dma('sync', (lambda c_: lambda e: e.dma_start(out=wstage[:, 0:255], in_=rtT[c_ * 128:(c_ + 1) * 128, :]))(c))
        p.op('vector', (lambda c_: lambda v: v.tensor_copy(rtb[c_][:, :], wstage[:, 0:255]))(c))
        p.dma('sync', (lambda c_: lambda e: e.dma_start(out=wstage[:, :], in_=woT[c_ * 128:(c_ + 1) * 128, :]))(c))
        p.op('vector', (lambda c_: lambda v: v.tensor_copy(wobt[c_][:, :], wstage[:, :]))(c))
    for c in range(2):
        p.dma('sync', (lambda c_: lambda e: e.dma_start(out=wstage[:, :], in_=lvw[c_ * 128:(c_ + 1) * 128, :]))(c))
        p.op('vector', (lambda c_: lambda v: v.tensor_copy(lvb[c_][:, :], wstage[:, :]))(c))
    p.dma('sync', lambda e: e.dma_start(out=wob_s[:, :], in_=wob))

    for tt in range(TH // 512):
        sl = slice(tt * 512, (tt + 1) * 512)
        p.dma('sync', *[(lambda c_, sl_: lambda e: e.dma_start(out=xnt[c_][:, :], in_=xnh[c_ * 128:(c_ + 1) * 128, sl_]))(c, sl) for c in range(8)])
        for ec in range(8):
            p.op('tensor', *[(lambda ec_, d_: lambda t: t.matmul(
                ps0[:, :], wqb[d_][:, ec_ * 128:(ec_ + 1) * 128], xnt[d_][:, :],
                start=(d_ == 0), stop=(d_ == 7)))(ec, d) for d in range(8)])
            p.op('vector', (lambda ec_: lambda v: v.tensor_copy(qb[ec_][:, :], ps0[:, :]))(ec))
        for tc in range(4):
            p.op('tensor', *[(lambda tc_, d_: lambda t: t.matmul(
                psg[:, :], qb[d_][:, tc_ * 128:(tc_ + 1) * 128], rtb[d_][:, :],
                start=(d_ == 0), stop=(d_ == 7)))(tc, d) for d in range(8)])
            p.op('scalar', lambda s: s.activation(gg[:, 255:510], psg[:, :], AF.Sigmoid))
            p.op('vector', lambda v: v.tensor_scalar(gg[:, 0:255], gg[:, 255:510], -1.0, 1.0, ALU.mult, ALU.add))
            for level in range(8):
                n = 1 << level
                half_ = 256 // (2 * n)
                fac = bass.AP(gg, n - 1, [[510, 128], [1, n], [255, 2], [0, half_]])
                lpv = lp[:, :].rearrange("p (a b c) -> p a b c", a=n, b=2, c=half_)
                if level == 0:
                    p.op('vector', (lambda lpv_, fac_: lambda v: v.tensor_copy(lpv_, fac_))(lpv, fac))
                else:
                    p.op('vector', (lambda lpv_, fac_: lambda v: v.tensor_tensor(lpv_, lpv_, fac_, ALU.mult))(lpv, fac))
            p.op('vector', lambda v: v.tensor_copy(lpb[:, :], lp[:, :]))
            p.op('tensor', *[(lambda mc_, tc_: lambda t: t.transpose(
                pst[:, mc_ * 128:(mc_ + 1) * 128], lpb[:, mc_ * 128:(mc_ + 1) * 128], ident[:, :]))(mc, tc) for mc in range(2)])
            p.op('vector', *[(lambda mc_, tc_: lambda v: v.tensor_copy(
                lpt[mc_][:, tc_ * 128:(tc_ + 1) * 128], pst[:, mc_ * 128:(mc_ + 1) * 128]))(mc, tc) for mc in range(2)])
        for dc in range(8):
            p.op('tensor', *[(lambda dc_, mc_: lambda t: t.matmul(
                ps0[:, :], lvb[mc_][:, dc_ * 128:(dc_ + 1) * 128], lpt[mc_][:, :],
                start=(mc_ == 0), stop=(mc_ == 1)))(dc, mc) for mc in range(2)])
            p.op('vector', (lambda dc_: lambda v: v.tensor_copy(retb[dc_][:, :], ps0[:, :]))(dc))
        for ec in range(8):
            p.op('tensor', *[(lambda ec_, d_: lambda t: t.matmul(
                ps0[:, :], wobt[d_][:, ec_ * 128:(ec_ + 1) * 128], retb[d_][:, :],
                start=(d_ == 0), stop=(d_ == 7)))(ec, d) for d in range(8)])
            p.op('scalar', (lambda ec_: lambda s: s.activation(g1[:, :], ps0[:, :], AF.Silu, bias=wob_s[:, ec_:ec_ + 1]))(ec))
            p.dma('sync',
                  (lambda ec_, sl_: lambda e: e.dma_start(out=x1t[:, :], in_=x1[ec_ * 128:(ec_ + 1) * 128, sl_]))(ec, sl),
                  (lambda ec_, sl_: lambda e: e.dma_start(out=s1t[:, :], in_=s1[ec_ * 128:(ec_ + 1) * 128, sl_]))(ec, sl))
            p.op('vector', lambda v: v.tensor_tensor(s1t[:, :], s1t[:, :], g1[:, :], ALU.mult))
            p.op('vector', lambda v: v.tensor_tensor(x1t[:, :], x1t[:, :], s1t[:, :], ALU.add))
            p.dma('gpsimd', (lambda ec_, sl_: lambda e: e.dma_start(out=x2o[ec_ * 128:(ec_ + 1) * 128, sl_], in_=x1t[:, :]))(ec, sl))
    return p


def stage8_l1ffn(xn1_slices, x1_slices, s1_slices, WqT2, routersT, LV, WoutT, wob_r):
    p = build_l1ffn2()
    in_maps = []
    for c in range(NC):
        in_maps.append({
            "xnh": xn1_slices[c], "x1": x1_slices[c], "s1": s1_slices[c],
            "wqT": WqT2, "rtT": routersT, "lv": LV, "woT": WoutT, "wob": wob_r,
        })
    res = _run(p, in_maps)
    return [res[c]["x2"] for c in range(NC)]


def kernel(L0_attn_hidden_states, L0_attn_Wk, L0_attn_bk, L0_attn_Wv, L0_attn_bv,
           L0_attn_ReadingHead, L0_attn_bidirection_weight, L0_ffn_W_fc, L0_ffn_W_proj,
           L1_attn_W_q, L1_attn_W_k, L1_attn_W_v, L1_attn_mask,
           L1_ffn_Wq, L1_ffn_routers, L1_ffn_leaf_keys, L1_ffn_leaf_values,
           L1_ffn_Wout_w, L1_ffn_Wout_b):
    ct = np.ascontiguousarray
    x_fm = ct(L0_attn_hidden_states.transpose(0, 2, 1))
    WfcT = ct(L0_ffn_W_fc.T)
    WprojT = ct(L0_ffn_W_proj.T)
    Wq1T = ct(L1_attn_W_q.T)
    Wk1T = ct(L1_attn_W_k.T)
    Wv1T = ct(L1_attn_W_v.T)
    Wq2T = ct(L1_ffn_Wq.T)
    routersT = ct(L1_ffn_routers.T)
    WoutT = ct(L1_ffn_Wout_w.T)
    wob_r = ct(L1_ffn_Wout_b.reshape(8, 128).T)

    xn0 = stage1_ln0(x_fm)
    v1_l, vld_l = stage2_l0mm(xn0, L0_attn_Wk, L0_attn_bk, L0_attn_Wv, L0_attn_bv,
                              L0_attn_ReadingHead)
    s_l = stage3_scan(v1_l, vld_l, L0_attn_bidirection_weight)
    g_l = stage4_ffn0(xn0, WfcT, WprojT)
    s_fm_slices = []
    for c in range(NC):
        b, half = c // 2, c % 2
        s_full_b = np.concatenate([s_l[2 * b].reshape(512, T), s_l[2 * b + 1].reshape(512, T)], axis=0)
        s_fm_slices.append(ct(s_full_b[:, half * TH:(half + 1) * TH]))
    x1_l, xn1_l = stage5_comb1(x_fm, s_fm_slices, g_l)
    qT_l, kT_l, v_l = stage6_qkv(xn1_l, Wq1T, Wk1T, Wv1T)
    qT_full = [np.concatenate([qT_l[2 * b], qT_l[2 * b + 1]], axis=1) for b in range(B)]
    v_full = [np.concatenate([v_l[2 * b], v_l[2 * b + 1]], axis=0) for b in range(B)]
    o_l = stage7_flash(kT_l, qT_full, v_full)
    s1_slices = [ct(o_l[c].T) for c in range(NC)]
    x2_l = stage8_l1ffn(xn1_l, x1_l, s1_slices, Wq2T, routersT, L1_ffn_leaf_values,
                        WoutT, wob_r)
    out = np.empty((B, D, T), np.float32)
    for c in range(NC):
        b, half = c // 2, c % 2
        out[b][:, half * TH:(half + 1) * TH] = x2_l[c]
    return ct(out.transpose(0, 2, 1))


# version 03
# speedup vs baseline: 1.0157x; 1.0157x over previous
import contextlib
import numpy as np
import concourse.bass as bass
import concourse.mybir as mybir
from concourse.bass_utils import run_bass_kernel_spmd

F32 = mybir.dt.float32
BF16 = mybir.dt.bfloat16
AF = mybir.ActivationFunctionType
ALU = mybir.AluOpType
AX = mybir.AxisListType

NC = 8
B, T, D = 4, 8192, 1024
TH = T // 2
H, HD, R = 16, 64, 4
FF = 4096
NEG = -3.0e38


class Prog:

    def __init__(self):
        self.nc = bass.Bass("TRN2", target_bir_lowering=False)
        self.es = contextlib.ExitStack()
        self.ops = []

    def din(self, name, shape, dt):
        return self.nc.dram_tensor(name, list(shape), dt, kind="ExternalInput").ap()

    def dout(self, name, shape, dt):
        return self.nc.dram_tensor(name, list(shape), dt, kind="ExternalOutput").ap()

    def sb(self, name, shape, dt):
        return self.es.enter_context(self.nc.sbuf_tensor(name, list(shape), dt))

    def ps(self, name, shape, dt=F32):
        return self.es.enter_context(self.nc.psum_tensor(name, list(shape), dt))

    def op(self, eng, *fns):
        self.ops.append((eng, list(fns), 'c'))

    def dma(self, eng, *fns):
        self.ops.append((eng, list(fns), 'd'))

    def finish(self):
        nc, es = self.nc, self.es
        C = es.enter_context(nc.semaphore("Cser"))
        Dm = es.enter_context(nc.semaphore("Dser"))
        block = es.enter_context(nc.Block())
        engines = []
        for e, _, _ in self.ops:
            if e not in engines:
                engines.append(e)
        if 'sync' not in engines:
            engines.append('sync')

        def make_body(ename):
            def body(eng):
                cn = dn = 0
                for (e, fns, kind) in self.ops:
                    if e == ename:
                        if cn:
                            eng.wait_ge(C, cn)
                        if dn:
                            eng.wait_ge(Dm, dn)
                        insts = [fn(eng) for fn in fns]
                        if kind == 'd':
                            for inst in insts:
                                inst.then_inc(Dm, 16)
                        else:
                            insts[-1].then_inc(C, 1)
                    if kind == 'd':
                        dn += 16 * len(fns)
                    else:
                        cn += 1
                if ename == 'sync':
                    eng.wait_ge(C, cn)
                    eng.wait_ge(Dm, dn)
            return body

        for ename in engines:
            getattr(block, ename)(make_body(ename))
        es.close()
        return nc


def _run(p, in_maps):
    nc = p.finish()
    res = run_bass_kernel_spmd(nc, in_maps, core_ids=list(range(NC)))
    return res.results


def build_ln(xname="x", oname="xn", width=TH, extra_out_x1=False):
    p = Prog()
    NT = width // 512
    x = p.din(xname, (D, width), F32)
    if extra_out_x1:
        s_in = p.din("s", (D, width), F32)
        g_in = p.din("g", (D, width), F32)
        x1_out = p.dout("x1", (D, width), F32)
    xn = p.dout(oname, (D, width), BF16)

    xt = [p.sb(f"xt{c}", (128, 512), F32) for c in range(8)]
    if extra_out_x1:
        st_ = [p.sb(f"st{c}", (128, 512), F32) for c in range(8)]
        gt = [p.sb(f"gt{c}", (128, 512), F32) for c in range(8)]
    xsq = p.sb("xsq", (128, 512), F32)
    xnb = [p.sb(f"xnb{c}", (128, 512), BF16) for c in range(8)]
    ones_col = p.sb("ones_col", (128, 1), F32)
    ones_row = p.sb("ones_row", (1, 128), F32)
    eps_col = p.sb("eps_col", (1, 1), F32)
    mrow = p.sb("mrow", (1, 512), F32)
    m2row = p.sb("m2row", (1, 512), F32)
    tmpr = p.sb("tmpr", (1, 512), F32)
    varr = p.sb("varr", (1, 512), F32)
    lnv = p.sb("lnv", (1, 512), F32)
    rrow = p.sb("rrow", (1, 512), F32)
    nmr = p.sb("nmr", (1, 512), F32)
    t1 = p.sb("t1", (128, 512), F32)
    ps_s = p.ps("ps_s", (1, 512))
    ps_q = p.ps("ps_q", (1, 512))
    ps_r = p.ps("ps_r", (128, 512))
    ps_n = p.ps("ps_n", (128, 512))

    p.op('gpsimd',
         lambda g: g.memset(ones_col[:, :], 1.0),
         lambda g: g.memset(ones_row[:, :], 1.0),
         lambda g: g.memset(eps_col[:, :], 1e-5))

    for tt in range(NT):
        sl = slice(tt * 512, (tt + 1) * 512)
        p.dma('sync', *[(lambda c_, sl_: lambda e: e.dma_start(out=xt[c_][:, :], in_=x[c_ * 128:(c_ + 1) * 128, sl_]))(c, sl) for c in range(8)])
        if extra_out_x1:
            p.dma('sync', *[(lambda c_, sl_: lambda e: e.dma_start(out=st_[c_][:, :], in_=s_in[c_ * 128:(c_ + 1) * 128, sl_]))(c, sl) for c in range(8)])
            p.dma('sync', *[(lambda c_, sl_: lambda e: e.dma_start(out=gt[c_][:, :], in_=g_in[c_ * 128:(c_ + 1) * 128, sl_]))(c, sl) for c in range(8)])
            p.op('vector', *[(lambda c_: lambda v: v.tensor_tensor(st_[c_][:, :], st_[c_][:, :], gt[c_][:, :], ALU.mult))(c) for c in range(8)])
            p.op('vector', *[(lambda c_: lambda v: v.tensor_tensor(xt[c_][:, :], xt[c_][:, :], st_[c_][:, :], ALU.add))(c) for c in range(8)])
            p.dma('sync', *[(lambda c_, sl_: lambda e: e.dma_start(out=x1_out[c_ * 128:(c_ + 1) * 128, sl_], in_=xt[c_][:, :]))(c, sl) for c in range(8)])
        p.op('tensor', *[(lambda c_: lambda t: t.matmul(ps_s[:, :], ones_col[:, :], xt[c_][:, :], start=(c_ == 0), stop=(c_ == 7)))(c) for c in range(8)])
        for c in range(8):
            p.op('scalar', (lambda c_: lambda s: s.activation(xsq[:, :], xt[c_][:, :], AF.Square))(c))
            p.op('tensor', (lambda c_: lambda t: t.matmul(ps_q[:, :], ones_col[:, :], xsq[:, :], start=(c_ == 0), stop=(c_ == 7)))(c))
        p.op('vector',
             lambda v: v.tensor_scalar(mrow[:, :], ps_s[:, :], 1.0 / D, None, ALU.mult),
             lambda v: v.tensor_scalar(m2row[:, :], ps_q[:, :], 1.0 / D, None, ALU.mult))
        p.op('vector', lambda v: v.tensor_tensor(tmpr[:, :], mrow[:, :], mrow[:, :], ALU.mult))
        p.op('vector', lambda v: v.tensor_tensor(varr[:, :], m2row[:, :], tmpr[:, :], ALU.subtract))
        p.op('scalar', lambda s: s.activation(lnv[:, :], varr[:, :], AF.Ln, bias=eps_col[:, :]))
        p.op('scalar', lambda s: s.activation(rrow[:, :], lnv[:, :], AF.Exp, scale=-0.5))
        p.op('vector', lambda v: v.scalar_tensor_tensor(nmr[:, :], mrow[:, :], -1.0, rrow[:, :], ALU.mult, ALU.mult))
        p.op('tensor',
             lambda t: t.matmul(ps_r[:, :], ones_row[:, :], rrow[:, :], start=True, stop=True),
             lambda t: t.matmul(ps_n[:, :], ones_row[:, :], nmr[:, :], start=True, stop=True))
        for c in range(8):
            p.op('vector', (lambda c_: lambda v: v.tensor_tensor(t1[:, :], xt[c_][:, :], ps_r[:, :], ALU.mult))(c))
            p.op('vector', (lambda c_: lambda v: v.tensor_tensor(xnb[c_][:, :], t1[:, :], ps_n[:, :], ALU.add))(c))
        p.dma('sync', *[(lambda c_, sl_: lambda e: e.dma_start(out=xn[c_ * 128:(c_ + 1) * 128, sl_], in_=xnb[c_][:, :]))(c, sl) for c in range(8)])
    return p


def stage1_ln0(x_fm):
    p = build_ln()
    in_maps = []
    for c in range(NC):
        b, half = c // 2, c % 2
        in_maps.append({"x": np.ascontiguousarray(x_fm[b][:, half * TH:(half + 1) * TH])})
    res = _run(p, in_maps)
    xn0 = np.empty((B, D, T), dtype=res[0]["xn"].dtype)
    for c in range(NC):
        b, half = c // 2, c % 2
        xn0[b][:, half * TH:(half + 1) * TH] = res[c]["xn"]
    return xn0


def build_l0mm():
    p = Prog()
    xn = p.din("xn", (D, T), BF16)
    wk = p.din("wk", (D, 512), F32)
    wv = p.din("wv", (D, 512), F32)
    bkr = p.din("bkr", (128, 4), F32)
    bvr = p.din("bvr", (128, 4), F32)
    dotw = p.din("dotw", (128, 512), F32)
    v1o = p.dout("v1", (4, 128, T), F32)
    vldo = p.dout("vld", (4, 128, T), BF16)

    xnt = [p.sb(f"xnt{c}", (128, T), BF16) for c in range(8)]
    wstage = p.sb("wstage", (128, 512), F32)
    wkb = [p.sb(f"wkb{c}", (128, 512), BF16) for c in range(8)]
    wvb = [p.sb(f"wvb{c}", (128, 512), BF16) for c in range(8)]
    bks = p.sb("bks", (128, 4), F32)
    bvs = p.sb("bvs", (128, 4), F32)
    dws = p.sb("dws", (128, 512), F32)
    k1 = [p.sb(f"k1_{c}", (128, 512), F32) for c in range(4)]
    v1s = [p.sb(f"v1s{c}", (128, 512), F32) for c in range(4)]
    vldb = [p.sb(f"vldb{c}", (128, 512), BF16) for c in range(4)]
    psKV = [p.ps(f"ps{i}", (128, 512)) for i in range(8)]

    p.dma('sync', *[(lambda c_: lambda e: e.dma_start(out=xnt[c_][:, :], in_=xn[c_ * 128:(c_ + 1) * 128, :]))(c) for c in range(8)])
    p.dma('sync',
          lambda e: e.dma_start(out=bks[:, :], in_=bkr),
          lambda e: e.dma_start(out=bvs[:, :], in_=bvr),
          lambda e: e.dma_start(out=dws[:, :], in_=dotw))
    for c in range(8):
        p.dma('sync', (lambda c_: lambda e: e.dma_start(out=wstage[:, :], in_=wk[c_ * 128:(c_ + 1) * 128, :]))(c))
        p.op('vector', (lambda c_: lambda v: v.tensor_copy(wkb[c_][:, :], wstage[:, :]))(c))
        p.dma('sync', (lambda c_: lambda e: e.dma_start(out=wstage[:, :], in_=wv[c_ * 128:(c_ + 1) * 128, :]))(c))
        p.op('vector', (lambda c_: lambda v: v.tensor_copy(wvb[c_][:, :], wstage[:, :]))(c))

    for tt in range(T // 512):
        sl = slice(tt * 512, (tt + 1) * 512)
        for c4 in range(4):
            p.op('tensor', *[(lambda c4_, d_, sl_: lambda t: t.matmul(
                psKV[c4_][:, :], wkb[d_][:, c4_ * 128:(c4_ + 1) * 128], xnt[d_][:, sl_],
                start=(d_ == 0), stop=(d_ == 7)))(c4, d, sl) for d in range(8)])
        for c4 in range(4):
            p.op('tensor', *[(lambda c4_, d_, sl_: lambda t: t.matmul(
                psKV[4 + c4_][:, :], wvb[d_][:, c4_ * 128:(c4_ + 1) * 128], xnt[d_][:, sl_],
                start=(d_ == 0), stop=(d_ == 7)))(c4, d, sl) for d in range(8)])
        p.op('scalar', *[(lambda c4_: lambda s: s.activation(k1[c4_][:, :], psKV[c4_][:, :], AF.Relu, bias=bks[:, c4_:c4_ + 1]))(c4) for c4 in range(4)])
        p.op('scalar', *[(lambda c4_: lambda s: s.activation(v1s[c4_][:, :], psKV[4 + c4_][:, :], AF.Relu, bias=bvs[:, c4_:c4_ + 1]))(c4) for c4 in range(4)])
        p.dma('sync', *[(lambda c4_, sl_: lambda e: e.dma_start(out=v1o[c4_, :, sl_], in_=v1s[c4_][:, :]))(c4, sl) for c4 in range(4)])
        p.op('tensor', *[(lambda c4_: lambda t: t.matmul(psKV[c4_][:, :], dws[:, c4_ * 128:(c4_ + 1) * 128], k1[c4_][:, :], start=True, stop=True))(c4) for c4 in range(4)])
        p.op('vector', *[(lambda c4_: lambda v: v.tensor_scalar(vldb[c4_][:, :], psKV[c4_][:, :], 0.5, None, ALU.is_gt))(c4) for c4 in range(4)])
        p.dma('sync', *[(lambda c4_, sl_: lambda e: e.dma_start(out=vldo[c4_, :, sl_], in_=vldb[c4_][:, :]))(c4, sl) for c4 in range(4)])
    return p


def stage2_l0mm(xn0, Wk, bk, Wv, bv, RH):
    p = build_l0mm()
    in_maps = []
    for c in range(NC):
        b, half = c // 2, c % 2
        hsl = slice(half * 512, (half + 1) * 512)
        dotw = np.zeros((128, 512), np.float32)
        for c4 in range(4):
            chunk = half * 4 + c4
            for j in range(2):
                head = 2 * chunk + j
                blk = np.broadcast_to(RH[head][:, None], (64, 64))
                dotw[j * 64:(j + 1) * 64, c4 * 128 + j * 64:c4 * 128 + (j + 1) * 64] = blk
        in_maps.append({
            "xn": np.ascontiguousarray(xn0[b]),
            "wk": np.ascontiguousarray(Wk[:, hsl]),
            "wv": np.ascontiguousarray(Wv[:, hsl]),
            "bkr": np.ascontiguousarray(bk[hsl].reshape(4, 128).T),
            "bvr": np.ascontiguousarray(bv[hsl].reshape(4, 128).T),
            "dotw": dotw,
        })
    res = _run(p, in_maps)
    return ([res[c]["v1"] for c in range(NC)], [res[c]["vld"] for c in range(NC)])


def build_scan():
    p = Prog()
    W = T + 2
    v1 = p.din("v1", (4, 128, T), F32)
    vld = p.din("vld", (4, 128, T), BF16)
    wrep = p.din("wrep", (4, 128, 8), F32)
    so = p.dout("s", (4, 128, T), F32)

    Vb = p.sb("Vb", (128, W), F32)
    Fa = p.sb("Fa", (128, W), F32)
    Fb = p.sb("Fb", (128, W), F32)
    bb = p.sb("bb", (128, T), F32)
    sacc = p.sb("sacc", (128, T), F32)
    valF = p.sb("valF", (128, T), BF16)
    nvalF = p.sb("nvalF", (128, T), BF16)
    wcols = p.sb("wcols", (128, 8), F32)
    wbsum = p.sb("wbsum", (128, 1), F32)
    c1 = p.sb("c1", (128, 1), F32)
    c2 = p.sb("c2", (128, 1), F32)
    initc = p.sb("initc", (128, 1), F32)
    s0col = p.sb("s0col", (128, 1), F32)

    def rev(t_, off, n):
        return bass.AP(t_, off, [[t_.shape[1], 128], [-1, n]])

    for ti in range(4):
        p.dma('sync',
              (lambda ti_: lambda e: e.dma_start(out=Vb[:, 1:T + 1], in_=v1[ti_, :, :]))(ti),
              (lambda ti_: lambda e: e.dma_start(out=valF[:, :], in_=vld[ti_, :, :]))(ti),
              (lambda ti_: lambda e: e.dma_start(out=wcols[:, :], in_=wrep[ti_, :, :]))(ti))
        p.op('gpsimd',
             lambda g: g.memset(Vb[:, 0:1], 0.0),
             lambda g: g.memset(Vb[:, T + 1:T + 2], 0.0),
             lambda g: g.memset(Fa[:, 0:1], 0.0),
             lambda g: g.memset(Fa[:, T + 1:T + 2], 0.0),
             lambda g: g.memset(Fb[:, 0:1], 0.0),
             lambda g: g.memset(Fb[:, T + 1:T + 2], 0.0),
             lambda g: g.memset(valF[:, 0:1], 0.0),
             lambda g: g.memset(sacc[:, :], 0.0))
        p.op('vector',
             lambda v: v.tensor_scalar(nvalF[:, :], valF[:, :], -1.0, 1.0, ALU.mult, ALU.add),
             lambda v: v.tensor_tensor(c1[:, :], valF[:, T - 1:T], Vb[:, T:T + 1], ALU.mult),
             lambda v: v.tensor_reduce(wbsum[:, :], wcols[:, 4:8], AX.X, ALU.add))
        p.op('vector', lambda v: v.tensor_tensor(c2[:, :], nvalF[:, T - 1:T], Vb[:, 1:2], ALU.mult))
        p.op('vector', lambda v: v.tensor_tensor(initc[:, :], c1[:, :], c2[:, :], ALU.add))
        src = {0: Vb, 1: None}
        bufs = [Fa, Fb, Fa, Fb]
        prev = None
        for r in range(4):
            if r == 0:
                p.op('vector', lambda v: v.tensor_tensor(bb[:, :], valF[:, :], Vb[:, 1:T + 1], ALU.mult))
            else:
                p.op('vector', (lambda pr: lambda v: v.tensor_tensor(bb[:, :], valF[:, :], pr[:, 0:T], ALU.mult))(prev))
            F = bufs[r]
            p.op('vector', (lambda F_: lambda v: v.tensor_tensor_scan(F_[:, 1:T + 1], nvalF[:, :], bb[:, :], Vb[:, 1:2], ALU.mult, ALU.add))(F))
            p.op('vector', (lambda F_, r_: lambda v: v.scalar_tensor_tensor(sacc[:, :], F_[:, 1:T + 1], wcols[:, r_:r_ + 1], sacc[:, :], ALU.mult, ALU.add))(F, r))
            prev = F
        p.op('vector', lambda v: v.tensor_copy(s0col[:, :], sacc[:, 0:1]))
        p.op('gpsimd',
             lambda g: g.memset(valF[:, T - 1:T], 0.0),
             lambda g: g.memset(nvalF[:, T - 1:T], 1.0))
        prev = None
        for r in range(4):
            if r == 0:
                p.op('vector', lambda v: v.tensor_tensor(bb[:, :], valF[:, :], Vb[:, 2:T + 2], ALU.mult))
                init_ap = initc[:, :]
            else:
                p.op('vector', (lambda pr: lambda v: v.tensor_tensor(bb[:, :], valF[:, :], pr[:, 2:T + 2], ALU.mult))(prev))
                init_ap = Vb[:, 1:2]
            F = bufs[r]
            p.op('vector', (lambda F_, ia: lambda v: v.tensor_tensor_scan(
                rev(F_, T, T), rev(nvalF, T - 1, T), rev(bb, T - 1, T), ia, ALU.mult, ALU.add))(F, init_ap))
            p.op('vector', (lambda F_, r_: lambda v: v.scalar_tensor_tensor(sacc[:, :], F_[:, 1:T + 1], wcols[:, 4 + r_:5 + r_], sacc[:, :], ALU.mult, ALU.add))(F, r))
            prev = F
        p.op('vector', lambda v: v.tensor_tensor(c1[:, :], wbsum[:, :], Vb[:, 1:2], ALU.mult))
        p.op('vector', lambda v: v.tensor_tensor(c2[:, :], c1[:, :], s0col[:, :], ALU.add))
        p.op('vector', lambda v: v.tensor_copy(sacc[:, 0:1], c2[:, :]))
        p.dma('sync', (lambda ti_: lambda e: e.dma_start(out=so[ti_, :, :], in_=sacc[:, :]))(ti))
    return p


def stage3_scan(v1_list, vld_list, bw):
    p = build_scan()
    in_maps = []
    for c in range(NC):
        b, half = c // 2, c % 2
        wrep = np.zeros((4, 128, 8), np.float32)
        for ti in range(4):
            chunk = half * 4 + ti
            for j in range(2):
                wrep[ti, j * 64:(j + 1) * 64, :] = bw[2 * chunk + j][None, :]
        in_maps.append({"v1": v1_list[c], "vld": vld_list[c], "wrep": wrep})
    res = _run(p, in_maps)
    return [res[c]["s"] for c in range(NC)]


def build_ffn0():
    p = Prog()
    xnh = p.din("xnh", (D, TH), BF16)
    wfcT = p.din("wfcT", (D, FF), F32)
    wprojT = p.din("wprojT", (FF, D), F32)
    go = p.dout("g", (D, TH), F32)

    wstage = p.sb("wstage", (128, FF), F32)
    wfcb = [p.sb(f"wfcb{c}", (128, FF), BF16) for c in range(8)]
    wprjb = [p.sb(f"wprjb{c}", (128, D), BF16) for c in range(32)]
    xnt = [p.sb(f"xnt{c}", (128, 512), BF16) for c in range(8)]
    h2 = [p.sb(f"h2_{c}", (128, 512), BF16) for c in range(32)]
    htmp = p.sb("htmp", (128, 512), F32)
    yd = p.sb("yd", (128, 512), F32)
    mem = p.sb("mem", (128, 512), F32)
    spk = p.sb("spk", (128, 512), F32)
    acc = p.sb("acc", (128, 512), F32)
    gd = p.sb("gd", (128, 512), F32)
    psH = p.ps("psH", (128, 512))
    psY = p.ps("psY", (128, 512))

    for c in range(8):
        p.dma('sync', (lambda c_: lambda e: e.dma_start(out=wstage[:, :], in_=wfcT[c_ * 128:(c_ + 1) * 128, :]))(c))
        p.op('vector', (lambda c_: lambda v: v.tensor_copy(wfcb[c_][:, :], wstage[:, :]))(c))
    for c in range(32):
        p.dma('sync', (lambda c_: lambda e: e.dma_start(out=wstage[:, 0:D], in_=wprojT[c_ * 128:(c_ + 1) * 128, :]))(c))
        p.op('vector', (lambda c_: lambda v: v.tensor_copy(wprjb[c_][:, :], wstage[:, 0:D]))(c))

    for tt in range(TH // 512):
        sl = slice(tt * 512, (tt + 1) * 512)
        p.dma('sync', *[(lambda c_, sl_: lambda e: e.dma_start(out=xnt[c_][:, :], in_=xnh[c_ * 128:(c_ + 1) * 128, sl_]))(c, sl) for c in range(8)])
        for fc in range(32):
            p.op('tensor', *[(lambda fc_, d_: lambda t: t.matmul(
                psH[:, :], wfcb[d_][:, fc_ * 128:(fc_ + 1) * 128], xnt[d_][:, :],
                start=(d_ == 0), stop=(d_ == 7)))(fc, d) for d in range(8)])
            p.op('scalar', (lambda: lambda s: s.activation(htmp[:, :], psH[:, :], AF.Relu))())
            p.op('scalar', (lambda fc_: lambda s: s.activation(h2[fc_][:, :], htmp[:, :], AF.Square))(fc))
        for dc in range(8):
            p.op('tensor', *[(lambda dc_, f_: lambda t: t.matmul(
                psY[:, :], wprjb[f_][:, dc_ * 128:(dc_ + 1) * 128], h2[f_][:, :],
                start=(f_ == 0), stop=(f_ == 31)))(dc, f) for f in range(32)])
            p.op('vector',
                 lambda v: v.tensor_copy(yd[:, :], psY[:, :]),
                 lambda v: v.tensor_copy(mem[:, :], psY[:, :]),
                 lambda v: v.tensor_scalar(spk[:, :], psY[:, :], 1.0, None, ALU.is_gt))
            p.op('vector', lambda v: v.tensor_copy(acc[:, :], spk[:, :]))
            for k in range(4):
                p.op('vector', lambda v: v.scalar_tensor_tensor(mem[:, :], mem[:, :], 0.9, yd[:, :], ALU.mult, ALU.add))
                p.op('vector', lambda v: v.tensor_tensor(mem[:, :], mem[:, :], spk[:, :], ALU.subtract))
                p.op('vector', lambda v: v.tensor_scalar(spk[:, :], mem[:, :], 1.0, None, ALU.is_gt))
                p.op('vector', lambda v: v.tensor_tensor(acc[:, :], acc[:, :], spk[:, :], ALU.add))
            p.op('scalar', lambda s: s.activation(gd[:, :], acc[:, :], AF.Silu, scale=0.2))
            p.dma('sync', (lambda dc_, sl_: lambda e: e.dma_start(out=go[dc_ * 128:(dc_ + 1) * 128, sl_], in_=gd[:, :]))(dc, sl))
    return p


def stage4_ffn0(xn0, WfcT, WprojT):
    p = build_ffn0()
    in_maps = []
    for c in range(NC):
        b, half = c // 2, c % 2
        in_maps.append({
            "xnh": np.ascontiguousarray(xn0[b][:, half * TH:(half + 1) * TH]),
            "wfcT": WfcT, "wprojT": WprojT,
        })
    res = _run(p, in_maps)
    return [res[c]["g"] for c in range(NC)]


def stage5_comb1(x_fm, s_fm_slices, g_slices):
    p = build_ln(oname="xn", extra_out_x1=True)
    in_maps = []
    for c in range(NC):
        b, half = c // 2, c % 2
        in_maps.append({
            "x": np.ascontiguousarray(x_fm[b][:, half * TH:(half + 1) * TH]),
            "s": s_fm_slices[c],
            "g": g_slices[c],
        })
    res = _run(p, in_maps)
    return ([res[c]["x1"] for c in range(NC)], [res[c]["xn"] for c in range(NC)])


def build_qkv():
    p = Prog()
    xnh = p.din("xnh", (D, TH), BF16)
    wqT = p.din("wqT", (D, FF), F32)
    wkT = p.din("wkT", (D, FF), F32)
    wvT = p.din("wvT", (D, D), F32)
    qTo = p.dout("qT", (FF, TH), BF16)
    kTo = p.dout("kT", (FF, TH), BF16)
    vo = p.dout("v", (TH, D), BF16)

    wstage = p.sb("wstage", (128, FF), F32)
    wqb = [p.sb(f"wqb{c}", (128, FF), BF16) for c in range(8)]
    wkb = [p.sb(f"wkb{c}", (128, FF), BF16) for c in range(8)]
    wvb = [p.sb(f"wvb{c}", (128, D), BF16) for c in range(8)]
    xnt = [p.sb(f"xnt{c}", (128, 512), BF16) for c in range(8)]
    ev = p.sb("ev", (128, 512), BF16)
    ps0 = p.ps("ps0", (128, 512))
    ps1 = p.ps("ps1", (128, 512))

    for c in range(8):
        p.dma('sync', (lambda c_: lambda e: e.dma_start(out=wstage[:, :], in_=wqT[c_ * 128:(c_ + 1) * 128, :]))(c))
        p.op('vector', (lambda c_: lambda v: v.tensor_copy(wqb[c_][:, :], wstage[:, :]))(c))
        p.dma('sync', (lambda c_: lambda e: e.dma_start(out=wstage[:, :], in_=wkT[c_ * 128:(c_ + 1) * 128, :]))(c))
        p.op('vector', (lambda c_: lambda v: v.tensor_copy(wkb[c_][:, :], wstage[:, :]))(c))
        p.dma('sync', (lambda c_: lambda e: e.dma_start(out=wstage[:, 0:D], in_=wvT[c_ * 128:(c_ + 1) * 128, :]))(c))
        p.op('vector', (lambda c_: lambda v: v.tensor_copy(wvb[c_][:, :], wstage[:, 0:D]))(c))

    for tt in range(TH // 512):
        sl = slice(tt * 512, (tt + 1) * 512)
        p.dma('sync', *[(lambda c_, sl_: lambda e: e.dma_start(out=xnt[c_][:, :], in_=xnh[c_ * 128:(c_ + 1) * 128, sl_]))(c, sl) for c in range(8)])
        for wb, out in ((wqb, qTo), (wkb, kTo)):
            for hc in range(32):
                ps = ps0 if hc % 2 == 0 else ps1
                p.op('tensor', *[(lambda ps_, wb_, hc_, d_: lambda t: t.matmul(
                    ps_[:, :], wb_[d_][:, hc_ * 128:(hc_ + 1) * 128], xnt[d_][:, :],
                    start=(d_ == 0), stop=(d_ == 7)))(ps, wb, hc, d) for d in range(8)])
                p.op('vector', (lambda ps_: lambda v: v.tensor_copy(ev[:, :], ps_[:, :]))(ps))
                p.dma('sync', (lambda out_, hc_, sl_: lambda e: e.dma_start(out=out_[hc_ * 128:(hc_ + 1) * 128, sl_], in_=ev[:, :]))(out, hc, sl))
        for tc in range(4):
            for oh in range(2):
                ps = ps0 if oh == 0 else ps1
                p.op('tensor', *[(lambda ps_, tc_, oh_, d_: lambda t: t.matmul(
                    ps_[:, :], xnt[d_][:, tc_ * 128:(tc_ + 1) * 128], wvb[d_][:, oh_ * 512:(oh_ + 1) * 512],
                    start=(d_ == 0), stop=(d_ == 7)))(ps, tc, oh, d) for d in range(8)])
                p.op('vector', (lambda ps_: lambda v: v.tensor_copy(ev[:, :], ps_[:, :]))(ps))
                p.dma('sync', (lambda tc_, oh_, sl_: lambda e: e.dma_start(
                    out=vo[sl_.start + tc_ * 128:sl_.start + (tc_ + 1) * 128, oh_ * 512:(oh_ + 1) * 512], in_=ev[:, :]))(tc, oh, sl))
    return p


def stage6_qkv(xn1_slices, WqT, WkT, WvT):
    p = build_qkv()
    in_maps = []
    for c in range(NC):
        in_maps.append({"xnh": xn1_slices[c], "wqT": WqT, "wkT": WkT, "wvT": WvT})
    res = _run(p, in_maps)
    return ([res[c]["qT"] for c in range(NC)], [res[c]["kT"] for c in range(NC)],
            [res[c]["v"] for c in range(NC)])


def build_flash():
    p = Prog()
    kT = p.din("kT", (FF, TH), BF16)
    qT = p.din("qT", (FF, T), BF16)
    vt = p.din("v", (T, D), BF16)
    oo = p.dout("o", (TH, D), F32)

    NG = 4
    GI = 8
    NJ = T // 512

    kts = [p.sb(f"kts{c}", (128, 1024), BF16) for c in range(32)]
    qts = [p.sb(f"qts{c}", (128, 512), BF16) for c in range(32)]
    vts = [p.sb(f"vts{c}", (128, 1024), BF16) for c in range(4)]
    Oacc = [p.sb(f"Oacc{i}", (128, 1024), F32) for i in range(GI)]
    mcols = p.sb("mcols", (128, GI), F32)
    lcols = p.sb("lcols", (128, GI), F32)
    mt = p.sb("mt", (128, 1), F32)
    mn = p.sb("mn", (128, 1), F32)
    dif = p.sb("dif", (128, 1), F32)
    bP = p.sb("bP", (128, 1), F32)
    acol = p.sb("acol", (128, 1), F32)
    rs = p.sb("rs", (128, 1), F32)
    linv = p.sb("linv", (128, 1), F32)
    p_sb = p.sb("p_sb", (128, 512), BF16)
    pt_sb = p.sb("pt_sb", (128, 512), BF16)
    ident = p.sb("ident", (128, 128), BF16)
    ident_i = p.sb("ident_i", (128, 128), mybir.dt.int32)
    ob = p.sb("ob", (128, 1024), F32)
    psS = p.ps("psS", (128, 512))
    psT = p.ps("psT", (128, 512), BF16)
    psO = [p.ps(f"psO{i}", (128, 512)) for i in range(2)]

    p.op('gpsimd', lambda g: g.iota(ident_i[:, :], [[1, 128]], base=0, channel_multiplier=-1))
    p.op('vector', lambda v: v.tensor_scalar(ident[:, :], ident_i[:, :], 0, None, ALU.is_equal))

    for gi in range(NG):
        p.dma('sync', *[(lambda c_, gi_: lambda e: e.dma_start(out=kts[c_][:, :], in_=kT[c_ * 128:(c_ + 1) * 128, gi_ * 1024:(gi_ + 1) * 1024]))(c, gi) for c in range(32)])
        p.op('gpsimd',
             *[(lambda i_: lambda g: g.memset(Oacc[i_][:, :], 0.0))(i) for i in range(GI)],
             lambda g: g.memset(mcols[:, :], NEG),
             lambda g: g.memset(lcols[:, :], 0.0))
        for j in range(NJ):
            p.dma('sync', *[(lambda c_, j_: lambda e: e.dma_start(out=qts[c_][:, :], in_=qT[c_ * 128:(c_ + 1) * 128, j_ * 512:(j_ + 1) * 512]))(c, j) for c in range(32)])
            p.dma('sync', *[(lambda c_, j_: lambda e: e.dma_start(out=vts[c_][:, :], in_=vt[j_ * 512 + c_ * 128:j_ * 512 + (c_ + 1) * 128, :]))(c, j) for c in range(4)])
            for i in range(GI):
                p.op('tensor', *[(lambda i_, hc_: lambda t: t.matmul(
                    psS[:, :], kts[hc_][:, i_ * 128:(i_ + 1) * 128], qts[hc_][:, :],
                    start=(hc_ == 0), stop=(hc_ == 31)))(i, hc) for hc in range(32)])
                p.op('vector', lambda v: v.tensor_reduce(mt[:, :], psS[:, :], AX.X, ALU.max))
                p.op('vector', (lambda i_: lambda v: v.tensor_tensor(mn[:, :], mcols[:, i_:i_ + 1], mt[:, :], ALU.max))(i))
                p.op('vector',
                     (lambda i_: lambda v: v.tensor_tensor(dif[:, :], mcols[:, i_:i_ + 1], mn[:, :], ALU.subtract))(i),
                     lambda v: v.tensor_scalar(bP[:, :], mn[:, :], -1.0 / 64.0, None, ALU.mult))
                p.op('vector', (lambda i_: lambda v: v.tensor_copy(mcols[:, i_:i_ + 1], mn[:, :]))(i))
                p.op('scalar',
                     lambda s: s.activation(acol[:, :], dif[:, :], AF.Exp, scale=1.0 / 64.0),
                     lambda s: s.activation(p_sb[:, :], psS[:, :], AF.Exp, bias=bP[:, :], scale=1.0 / 64.0, accum_out=rs[:, :]))
                p.op('vector', (lambda i_: lambda v: v.scalar_tensor_tensor(
                    lcols[:, i_:i_ + 1], lcols[:, i_:i_ + 1], acol[:, :], rs[:, :], ALU.mult, ALU.add))(i))
                p.op('tensor', *[(lambda jc_: lambda t: t.transpose(
                    psT[:, jc_ * 128:(jc_ + 1) * 128], p_sb[:, jc_ * 128:(jc_ + 1) * 128], ident[:, :]))(jc) for jc in range(4)])
                p.op('vector', lambda v: v.tensor_copy(pt_sb[:, :], psT[:, :]))
                p.op('tensor', *[(lambda oh_, jc_: lambda t: t.matmul(
                    psO[oh_][:, :], pt_sb[:, jc_ * 128:(jc_ + 1) * 128], vts[jc_][:, oh_ * 512:(oh_ + 1) * 512],
                    start=(jc_ == 0), stop=(jc_ == 3)))(oh, jc) for oh in range(2) for jc in range(4)])
                p.op('vector', *[(lambda i_, oh_: lambda v: v.scalar_tensor_tensor(
                    Oacc[i_][:, oh_ * 512:(oh_ + 1) * 512], Oacc[i_][:, oh_ * 512:(oh_ + 1) * 512],
                    acol[:, :], psO[oh_][:, :], ALU.mult, ALU.add))(i, oh) for oh in range(2)])
        for i in range(GI):
            p.op('vector', (lambda i_: lambda v: v.reciprocal(linv[:, :], lcols[:, i_:i_ + 1]))(i))
            p.op('scalar', (lambda i_: lambda s: s.activation(ob[:, :], Oacc[i_][:, :], AF.Identity, scale=linv[:, :]))(i))
            p.dma('sync', (lambda gi_, i_: lambda e: e.dma_start(out=oo[gi_ * 1024 + i_ * 128:gi_ * 1024 + (i_ + 1) * 128, :], in_=ob[:, :]))(gi, i))
    return p


def stage7_flash(kT_slices, qT_full, v_full):
    p = build_flash()
    in_maps = []
    for c in range(NC):
        b = c // 2
        in_maps.append({"kT": kT_slices[c], "qT": qT_full[b], "v": v_full[b]})
    res = _run(p, in_maps)
    return [res[c]["o"] for c in range(NC)]


def build_l1ffn2():
    p = Prog()
    xnh = p.din("xnh", (D, TH), BF16)
    x1 = p.din("x1", (D, TH), F32)
    s1 = p.din("s1", (D, TH), F32)
    wqT = p.din("wqT", (D, D), F32)
    rtT = p.din("rtT", (D, 255), F32)
    lvw = p.din("lv", (256, D), F32)
    woT = p.din("woT", (D, D), F32)
    wob = p.din("wob", (128, 8), F32)
    x2o = p.dout("x2", (D, TH), F32)

    wstage = p.sb("wstage", (128, D), F32)
    wqb = [p.sb(f"wqb{c}", (128, D), BF16) for c in range(8)]
    rtb = [p.sb(f"rtb{c}", (128, 255), BF16) for c in range(8)]
    lvb = [p.sb(f"lvb{c}", (128, D), BF16) for c in range(2)]
    wob_s = p.sb("wob_s", (128, 8), F32)
    wobt = [p.sb(f"wobt{c}", (128, D), BF16) for c in range(8)]
    xnt = [p.sb(f"xnt{c}", (128, 512), BF16) for c in range(8)]
    qb = [p.sb(f"qb{c}", (128, 512), BF16) for c in range(8)]
    gg = p.sb("gg", (128, 510), F32)
    lp = p.sb("lp", (128, 256), F32)
    lpb = p.sb("lpb", (128, 256), BF16)
    lpt = [p.sb(f"lpt{m}", (128, 512), BF16) for m in range(2)]
    retb = [p.sb(f"retb{c}", (128, 512), BF16) for c in range(8)]
    g1 = p.sb("g1", (128, 512), F32)
    x1t = p.sb("x1t", (128, 512), F32)
    s1t = p.sb("s1t", (128, 512), F32)
    ident = p.sb("ident", (128, 128), BF16)
    ident_i = p.sb("ident_i", (128, 128), mybir.dt.int32)
    ps0 = p.ps("ps0", (128, 512))
    psg = p.ps("psg", (128, 255))
    pst = p.ps("pst", (128, 512), BF16)

    p.op('gpsimd', lambda g: g.iota(ident_i[:, :], [[1, 128]], base=0, channel_multiplier=-1))
    p.op('vector', lambda v: v.tensor_scalar(ident[:, :], ident_i[:, :], 0, None, ALU.is_equal))
    for c in range(8):
        p.dma('sync', (lambda c_: lambda e: e.dma_start(out=wstage[:, :], in_=wqT[c_ * 128:(c_ + 1) * 128, :]))(c))
        p.op('vector', (lambda c_: lambda v: v.tensor_copy(wqb[c_][:, :], wstage[:, :]))(c))
        p.dma('sync', (lambda c_: lambda e: e.dma_start(out=wstage[:, 0:255], in_=rtT[c_ * 128:(c_ + 1) * 128, :]))(c))
        p.op('vector', (lambda c_: lambda v: v.tensor_copy(rtb[c_][:, :], wstage[:, 0:255]))(c))
        p.dma('sync', (lambda c_: lambda e: e.dma_start(out=wstage[:, :], in_=woT[c_ * 128:(c_ + 1) * 128, :]))(c))
        p.op('vector', (lambda c_: lambda v: v.tensor_copy(wobt[c_][:, :], wstage[:, :]))(c))
    for c in range(2):
        p.dma('sync', (lambda c_: lambda e: e.dma_start(out=wstage[:, :], in_=lvw[c_ * 128:(c_ + 1) * 128, :]))(c))
        p.op('vector', (lambda c_: lambda v: v.tensor_copy(lvb[c_][:, :], wstage[:, :]))(c))
    p.dma('sync', lambda e: e.dma_start(out=wob_s[:, :], in_=wob))

    for tt in range(TH // 512):
        sl = slice(tt * 512, (tt + 1) * 512)
        p.dma('sync', *[(lambda c_, sl_: lambda e: e.dma_start(out=xnt[c_][:, :], in_=xnh[c_ * 128:(c_ + 1) * 128, sl_]))(c, sl) for c in range(8)])
        for ec in range(8):
            p.op('tensor', *[(lambda ec_, d_: lambda t: t.matmul(
                ps0[:, :], wqb[d_][:, ec_ * 128:(ec_ + 1) * 128], xnt[d_][:, :],
                start=(d_ == 0), stop=(d_ == 7)))(ec, d) for d in range(8)])
            p.op('vector', (lambda ec_: lambda v: v.tensor_copy(qb[ec_][:, :], ps0[:, :]))(ec))
        for tc in range(4):
            p.op('tensor', *[(lambda tc_, d_: lambda t: t.matmul(
                psg[:, :], qb[d_][:, tc_ * 128:(tc_ + 1) * 128], rtb[d_][:, :],
                start=(d_ == 0), stop=(d_ == 7)))(tc, d) for d in range(8)])
            p.op('scalar', lambda s: s.activation(gg[:, 255:510], psg[:, :], AF.Sigmoid))
            p.op('vector', lambda v: v.tensor_scalar(gg[:, 0:255], gg[:, 255:510], -1.0, 1.0, ALU.mult, ALU.add))
            for level in range(8):
                n = 1 << level
                half_ = 256 // (2 * n)
                fac = bass.AP(gg, n - 1, [[510, 128], [1, n], [255, 2], [0, half_]])
                lpv = lp[:, :].rearrange("p (a b c) -> p a b c", a=n, b=2, c=half_)
                if level == 0:
                    p.op('vector', (lambda lpv_, fac_: lambda v: v.tensor_copy(lpv_, fac_))(lpv, fac))
                else:
                    p.op('vector', (lambda lpv_, fac_: lambda v: v.tensor_tensor(lpv_, lpv_, fac_, ALU.mult))(lpv, fac))
            p.op('vector', lambda v: v.tensor_copy(lpb[:, :], lp[:, :]))
            p.op('tensor', *[(lambda mc_, tc_: lambda t: t.transpose(
                pst[:, mc_ * 128:(mc_ + 1) * 128], lpb[:, mc_ * 128:(mc_ + 1) * 128], ident[:, :]))(mc, tc) for mc in range(2)])
            p.op('vector', *[(lambda mc_, tc_: lambda v: v.tensor_copy(
                lpt[mc_][:, tc_ * 128:(tc_ + 1) * 128], pst[:, mc_ * 128:(mc_ + 1) * 128]))(mc, tc) for mc in range(2)])
        for dc in range(8):
            p.op('tensor', *[(lambda dc_, mc_: lambda t: t.matmul(
                ps0[:, :], lvb[mc_][:, dc_ * 128:(dc_ + 1) * 128], lpt[mc_][:, :],
                start=(mc_ == 0), stop=(mc_ == 1)))(dc, mc) for mc in range(2)])
            p.op('vector', (lambda dc_: lambda v: v.tensor_copy(retb[dc_][:, :], ps0[:, :]))(dc))
        for ec in range(8):
            p.op('tensor', *[(lambda ec_, d_: lambda t: t.matmul(
                ps0[:, :], wobt[d_][:, ec_ * 128:(ec_ + 1) * 128], retb[d_][:, :],
                start=(d_ == 0), stop=(d_ == 7)))(ec, d) for d in range(8)])
            p.op('scalar', (lambda ec_: lambda s: s.activation(g1[:, :], ps0[:, :], AF.Silu, bias=wob_s[:, ec_:ec_ + 1]))(ec))
            p.dma('sync',
                  (lambda ec_, sl_: lambda e: e.dma_start(out=x1t[:, :], in_=x1[ec_ * 128:(ec_ + 1) * 128, sl_]))(ec, sl),
                  (lambda ec_, sl_: lambda e: e.dma_start(out=s1t[:, :], in_=s1[ec_ * 128:(ec_ + 1) * 128, sl_]))(ec, sl))
            p.op('vector', lambda v: v.tensor_tensor(s1t[:, :], s1t[:, :], g1[:, :], ALU.mult))
            p.op('vector', lambda v: v.tensor_tensor(x1t[:, :], x1t[:, :], s1t[:, :], ALU.add))
            p.dma('sync', (lambda ec_, sl_: lambda e: e.dma_start(out=x2o[ec_ * 128:(ec_ + 1) * 128, sl_], in_=x1t[:, :]))(ec, sl))
    return p


def stage8_l1ffn(xn1_slices, x1_slices, s1_slices, WqT2, routersT, LV, WoutT, wob_r):
    p = build_l1ffn2()
    in_maps = []
    for c in range(NC):
        in_maps.append({
            "xnh": xn1_slices[c], "x1": x1_slices[c], "s1": s1_slices[c],
            "wqT": WqT2, "rtT": routersT, "lv": LV, "woT": WoutT, "wob": wob_r,
        })
    res = _run(p, in_maps)
    return [res[c]["x2"] for c in range(NC)]


def kernel(L0_attn_hidden_states, L0_attn_Wk, L0_attn_bk, L0_attn_Wv, L0_attn_bv,
           L0_attn_ReadingHead, L0_attn_bidirection_weight, L0_ffn_W_fc, L0_ffn_W_proj,
           L1_attn_W_q, L1_attn_W_k, L1_attn_W_v, L1_attn_mask,
           L1_ffn_Wq, L1_ffn_routers, L1_ffn_leaf_keys, L1_ffn_leaf_values,
           L1_ffn_Wout_w, L1_ffn_Wout_b):
    ct = np.ascontiguousarray
    x_fm = ct(L0_attn_hidden_states.transpose(0, 2, 1))
    WfcT = ct(L0_ffn_W_fc.T)
    WprojT = ct(L0_ffn_W_proj.T)
    Wq1T = ct(L1_attn_W_q.T)
    Wk1T = ct(L1_attn_W_k.T)
    Wv1T = ct(L1_attn_W_v.T)
    Wq2T = ct(L1_ffn_Wq.T)
    routersT = ct(L1_ffn_routers.T)
    WoutT = ct(L1_ffn_Wout_w.T)
    wob_r = ct(L1_ffn_Wout_b.reshape(8, 128).T)

    xn0 = stage1_ln0(x_fm)
    v1_l, vld_l = stage2_l0mm(xn0, L0_attn_Wk, L0_attn_bk, L0_attn_Wv, L0_attn_bv,
                              L0_attn_ReadingHead)
    s_l = stage3_scan(v1_l, vld_l, L0_attn_bidirection_weight)
    g_l = stage4_ffn0(xn0, WfcT, WprojT)
    s_fm_slices = []
    for c in range(NC):
        b, half = c // 2, c % 2
        s_full_b = np.concatenate([s_l[2 * b].reshape(512, T), s_l[2 * b + 1].reshape(512, T)], axis=0)
        s_fm_slices.append(ct(s_full_b[:, half * TH:(half + 1) * TH]))
    x1_l, xn1_l = stage5_comb1(x_fm, s_fm_slices, g_l)
    qT_l, kT_l, v_l = stage6_qkv(xn1_l, Wq1T, Wk1T, Wv1T)
    qT_full = [np.concatenate([qT_l[2 * b], qT_l[2 * b + 1]], axis=1) for b in range(B)]
    v_full = [np.concatenate([v_l[2 * b], v_l[2 * b + 1]], axis=0) for b in range(B)]
    o_l = stage7_flash(kT_l, qT_full, v_full)
    s1_slices = [ct(o_l[c].T) for c in range(NC)]
    x2_l = stage8_l1ffn(xn1_l, x1_l, s1_slices, Wq2T, routersT, L1_ffn_leaf_values,
                        WoutT, wob_r)
    out = np.empty((B, D, T), np.float32)
    for c in range(NC):
        b, half = c // 2, c % 2
        out[b][:, half * TH:(half + 1) * TH] = x2_l[c]
    return ct(out.transpose(0, 2, 1))
